# version 25
# baseline (speedup 1.0000x reference)
import numpy as np
import ml_dtypes
import concourse.bass as bass
import concourse.mybir as mybir
from concourse.bass_utils import run_bass_kernel_spmd
from contextlib import ExitStack

F32 = mybir.dt.float32
BF16 = mybir.dt.bfloat16
F16 = mybir.dt.float16
I32 = mybir.dt.int32
I8 = mybir.dt.int8
AF = mybir.ActivationFunctionType
ALU = mybir.AluOpType
AX = mybir.AxisListType

P = 128
NT = 32
NK = 16
NH = 16
D = 2048
T = 4096
GQ = 4
NG = NT // GQ

DEBUG_SCRATCH = False
PHASES = ("A", "B", "V", "C", "F")


class Sem:

    def __init__(self, nc, es, name):
        self.h = es.enter_context(nc.semaphore(name))
        self.n = 0

    def inc(self, instr, k=1):
        instr.then_inc(self.h, k)
        self.n += k
        return self.n


def build_program(debug=DEBUG_SCRATCH, phases=PHASES):
    nc = bass.Bass("TRN2", target_bir_lowering=False)
    scr_kind = "ExternalOutput" if debug else "Internal"

    x_d = nc.dram_tensor("x", [T, D], F32, kind="ExternalInput").ap()
    wq_d = nc.dram_tensor("wq", [32, P, D], F32, kind="ExternalInput").ap()
    wv_d = nc.dram_tensor("wv", [NK, P, D], F32, kind="ExternalInput").ap()
    wo_d = nc.dram_tensor("wo", [D, D], F32, kind="ExternalInput").ap()
    wf_d = nc.dram_tensor("wf", [D, D], I8, kind="ExternalInput").ap()
    bqkc_d = nc.dram_tensor("bqkc", [P, 32], F32, kind="ExternalInput").ap()
    bqvc_d = nc.dram_tensor("bqvc", [P, NH], F32, kind="ExternalInput").ap()
    bout_d = nc.dram_tensor("bout", [1, D], F32, kind="ExternalInput").ap()
    wsc_d = nc.dram_tensor("wsc", [1, D], F32, kind="ExternalInput").ap()
    fbi_d = nc.dram_tensor("fbi", [1, D], F32, kind="ExternalInput").ap()
    idn_d = nc.dram_tensor("idn", [P, P], BF16, kind="ExternalInput").ap()
    tri_d = nc.dram_tensor("tri", [P, P], BF16, kind="ExternalInput").ap()
    onc_d = nc.dram_tensor("onc", [P, 1], BF16, kind="ExternalInput").ap()
    onm_d = nc.dram_tensor("onm", [P, P], BF16, kind="ExternalInput").ap()
    onr_d = nc.dram_tensor("onr", [1, 512], F32, kind="ExternalInput").ap()
    oni_d = nc.dram_tensor("oni", [1, P], F32, kind="ExternalInput").ap()
    onmf_d = nc.dram_tensor("onmf", [P, P], F32, kind="ExternalInput").ap()
    out_d = nc.dram_tensor("out", [T, D], F32, kind="ExternalOutput").ap()

    xt2_d = nc.dram_tensor("xt2", [NT, P, D], BF16, kind=scr_kind).ap()
    qt2_d = nc.dram_tensor("qt2", [NT, P, D], BF16, kind=scr_kind).ap()
    qth_d = nc.dram_tensor("qth", [NH, P, T], BF16, kind=scr_kind).ap()
    kth_d = nc.dram_tensor("kth", [NH, P, T], BF16, kind=scr_kind).ap()
    vn2_d = nc.dram_tensor("vn2", [NT, P, D], BF16, kind=scr_kind).ap()
    g2_d = nc.dram_tensor("g2", [NT, P, D], F16, kind=scr_kind).ap()
    yt_d = nc.dram_tensor("yt", [NH, P, T], BF16, kind=scr_kind).ap()
    is_d = nc.dram_tensor("isd", [P, NT], F32, kind=scr_kind).ap()

    es = ExitStack()
    idn_sb = es.enter_context(nc.sbuf_tensor("idn_sb", [P, P], BF16))
    tri_sb = es.enter_context(nc.sbuf_tensor("tri_sb", [P, P], BF16))
    onc_sb = es.enter_context(nc.sbuf_tensor("onc_sb", [P, 1], BF16))
    onm_sb = es.enter_context(nc.sbuf_tensor("onm_sb", [P, P], BF16))
    onr_sb = es.enter_context(nc.sbuf_tensor("onr_sb", [1, 512], F32))
    oni_sb = es.enter_context(nc.sbuf_tensor("oni_sb", [1, P], F32))
    onmf_sb = es.enter_context(nc.sbuf_tensor("onmf_sb", [P, P], F32))
    bqvc_sb = es.enter_context(nc.sbuf_tensor("bqvc_sb", [P, NH], F32))
    is_sb = es.enter_context(nc.sbuf_tensor("is_sb", [P, NT], F32))

    blk = es.enter_context(nc.Block())
    marks = {}
    cst = Sem(nc, es, "cst")

    @blk.sync
    def _(sy):
        for dst, srcd in ((idn_sb, idn_d), (tri_sb, tri_d), (onc_sb, onc_d),
                          (onm_sb, onm_d), (onr_sb, onr_d), (oni_sb, oni_d),
                          (onmf_sb, onmf_d), (bqvc_sb, bqvc_d)):
            cst.inc(sy.dma_start(out=dst[:], in_=srcd[:, :]), 16)
    cst_done = cst.n

    def barrier(sem, engines):
        for engname in engines:
            @getattr(blk, engname)
            def _(eng):
                sem.inc(eng.nop())
        return sem.n

    esA = ExitStack()
    xt_sb = esA.enter_context(nc.sbuf_tensor("xt_sb", [P, NK * T], BF16))
    barA = Sem(nc, es, "barA")
    a_wx = [Sem(nc, es, f"a_wx{j}") for j in range(2)]
    a_wq = [Sem(nc, es, f"a_wq{j}") for j in range(2)]
    barA_n = 0

    if "A" in phases:
        esA2 = ExitStack()
        xf = [esA2.enter_context(nc.sbuf_tensor(f"xf{j}", [P, D], F32)) for j in range(2)]
        xnf = [esA2.enter_context(nc.sbuf_tensor(f"xnf{j}", [P, D], F32)) for j in range(2)]
        qi_sb = esA2.enter_context(nc.sbuf_tensor("qi_sb", [P, D], I32))
        xnbt = [esA2.enter_context(nc.sbuf_tensor(f"xnbt{j}", [P, D], BF16)) for j in range(2)]
        qbt = [esA2.enter_context(nc.sbuf_tensor(f"qbt{j}", [P, D], BF16)) for j in range(2)]
        xnT_st = [esA2.enter_context(nc.sbuf_tensor(f"xnT{j}", [P, D], BF16)) for j in range(2)]
        qT_st = [esA2.enter_context(nc.sbuf_tensor(f"qT{j}", [P, D], BF16)) for j in range(2)]
        bn6 = [esA2.enter_context(nc.sbuf_tensor(f"bn6_{j}", [P, 24], F32)) for j in range(2)]
        stt = [[esA2.enter_context(nc.sbuf_tensor(f"st{j}_{k}", [P, 1], F32))
                for k in range(6)] for j in range(2)]
        agg = [esA2.enter_context(nc.sbuf_tensor(f"agg{j}", [P, 2], F32)) for j in range(2)]


        a_xld = [Sem(nc, es, f"a_xld{j}") for j in range(2)]
        a_tx = [Sem(nc, es, f"a_tx{j}") for j in range(2)]
        a_tq = [Sem(nc, es, f"a_tq{j}") for j in range(2)]
        a_s = Sem(nc, es, "a_s")
        a_v = Sem(nc, es, "a_v")
        a_t = Sem(nc, es, "a_t")
        VARP, RSTD, NMRS, AM = range(4)
        xt_view = xt_sb[:].rearrange("p (k t) -> p k t", k=NK)

        def sA(i, k):
            return 6 * i + k

        for i in range(NT):
            b = i % 2

            @blk.sync
            def _(sy):
                if i >= 2:
                    sy.wait_ge(a_s.h, sA(i - 2, 4))
                marks[("xld", i)] = a_xld[b].inc(
                    sy.dma_start(out=xf[b][:], in_=x_d[i * P:(i + 1) * P, :]), 16)

            @blk.vector
            def _(v):
                st = stt[b]
                v.wait_ge(a_xld[b].h, marks[("xld", i)])
                if i >= 2:
                    v.wait_ge(a_s.h, sA(i - 2, 6))
                for k in range(4):
                    ins = v.bn_stats(bn6[b][:, k * 6:(k + 1) * 6],
                                     xf[b][:, k * 512:(k + 1) * 512])
                n = a_v.inc(ins)
                v.wait_ge(a_v.h, n)
                n = a_v.inc(v.bn_aggr(agg[b][:], bn6[b][:]))
                v.wait_ge(a_v.h, n)
                n = a_v.inc(v.tensor_scalar_add(st[VARP][:], agg[b][:, 1:2], 1e-5))
                marks[("v_stat", i)] = a_v.inc(
                    v.tensor_scalar_mul(st[NMRS][:], agg[b][:, 0:1], -1.0))
                v.wait_ge(a_s.h, sA(i, 4))
                n = a_v.inc(v.reduce_max(st[AM][:], xnf[b][:], axis=AX.X,
                                         apply_absolute_value=True))
                v.wait_ge(a_v.h, n)
                n = a_v.inc(v.tensor_scalar(is_sb[:, i:i + 1], in0=st[AM][:],
                                            scalar1=1e-7, scalar2=1.0 / 127.0,
                                            op0=ALU.max, op1=ALU.mult))
                v.wait_ge(a_v.h, n)
                marks[("v_rsc", i)] = a_v.inc(
                    v.reciprocal(st[AM][:], is_sb[:, i:i + 1]))
                v.wait_ge(a_s.h, sA(i, 6))
                if i >= 2:
                    v.wait_ge(a_tq[b].h, marks[("tq", i - 2)])
                marks[("v_qb", i)] = a_v.inc(v.tensor_copy(qbt[b][:], qi_sb[:]))

            @blk.sync
            def _(sy):
                sy.wait_ge(a_s.h, sA(i, 5))
                if i >= 2:
                    sy.wait_ge(a_wx[b].h, marks[("g_wx", i - 2)])
                    sy.wait_ge(a_v.h, marks[("v_evx", 2 * (i - 2) + 1)])
                ins = None
                for k in range(NK):
                    ins = sy.dma_start(out=xnT_st[b][:, k * P:(k + 1) * P],
                                       in_=xnbt[b][:, k * P:(k + 1) * P],
                                       transpose=True)
                    a_tx[b].inc(ins, 16)
                marks[("tx", i)] = a_tx[b].n
                sy.wait_ge(a_v.h, marks[("v_qb", i)])
                if i >= 2:
                    sy.wait_ge(a_wq[b].h, marks[("g_wq", i - 2)])
                for k in range(NK):
                    ins = sy.dma_start(out=qT_st[b][:, k * P:(k + 1) * P],
                                       in_=qbt[b][:, k * P:(k + 1) * P],
                                       transpose=True)
                    a_tq[b].inc(ins, 16)
                marks[("tq", i)] = a_tq[b].n

            @blk.vector
            def _(v):
                v.wait_ge(a_tx[b].h, marks[("tx", i)])
                for half in range(2):
                    idx = 2 * i + half
                    dst = xt_view[:, half * 8:(half + 1) * 8, i * P:(i + 1) * P]
                    srcv = xnT_st[b][:, half * 1024:(half + 1) * 1024].rearrange(
                        "p (k t) -> p k t", k=8)
                    marks[("v_evx", idx)] = a_v.inc(v.tensor_copy(dst, srcv))

            @blk.gpsimd
            def _(g):
                g.wait_ge(a_tx[b].h, marks[("tx", i)])
                marks[("g_wx", i)] = a_wx[b].inc(
                    g.dma_start(out=xt2_d[i, :, :], in_=xnT_st[b][:]), 16)
                g.wait_ge(a_tq[b].h, marks[("tq", i)])
                marks[("g_wq", i)] = a_wq[b].inc(
                    g.dma_start(out=qt2_d[i, :, :], in_=qT_st[b][:]), 16)

        @blk.scalar
        def _(s):
            for i in range(NT):
                b = i % 2
                st = stt[b]
                s.wait_ge(a_v.h, marks[("v_stat", i)])
                n = a_s.inc(s.activation(st[RSTD][:], st[VARP][:], AF.Ln))
                s.wait_ge(a_s.h, n)
                n = a_s.inc(s.activation(st[RSTD][:], st[RSTD][:], AF.Exp, scale=-0.5))
                s.wait_ge(a_s.h, n)
                n = a_s.inc(s.activation(st[NMRS][:], st[NMRS][:], AF.Identity,
                                         scale=st[RSTD][:]))
                s.wait_ge(a_s.h, n)
                if i >= 2:
                    s.wait_ge(a_v.h, marks[("v_rsc", i - 2)])
                n = a_s.inc(s.activation(xnf[b][:], xf[b][:], AF.Identity,
                                         bias=st[NMRS][:], scale=st[RSTD][:]))
                assert n == sA(i, 4)
                if i >= 2:
                    s.wait_ge(a_tx[b].h, marks[("tx", i - 2)])
                a_s.inc(s.copy(xnbt[b][:], xnf[b][:]))
                s.wait_ge(a_v.h, marks[("v_rsc", i)])
                if i >= 1:
                    s.wait_ge(a_v.h, marks[("v_qb", i - 1)])
                n = a_s.inc(s.activation(qi_sb[:], xnf[b][:], AF.Identity,
                                         scale=st[AM][:]))
                assert n == sA(i, 6)

        if debug:
            @blk.sync
            def _(sy):
                sy.wait_ge(a_v.h, marks[("v_rsc", NT - 1)])
                cst.inc(sy.dma_start(out=is_d[:, :], in_=is_sb[:]), 16)
                sy.wait_ge(cst.h, cst.n)

        barA_n = barrier(barA, ("vector", "scalar", "tensor", "gpsimd"))
        esA2.close()

    barB = Sem(nc, es, "barB")
    barB_n = 0
    if "B" in phases:
        esB = ExitStack()
        wqb = [esB.enter_context(nc.sbuf_tensor(f"wqb{j}", [P, D], BF16)) for j in range(2)]
        stg = [esB.enter_context(nc.sbuf_tensor(f"stg{j}", [P, D], BF16)) for j in range(2)]
        psB = [esB.enter_context(nc.psum_tensor(f"psB{j}", [P, D], F32)) for j in range(2)]
        bqkc_sb = esB.enter_context(nc.sbuf_tensor("bqkc_sb", [P, 32], F32))
        bqkc2_sb = esB.enter_context(nc.sbuf_tensor("bqkc2_sb", [P, 32], F32))

        @blk.sync
        def _(sy):
            sy.wait_ge(barA.h, barA_n)
            cst.inc(sy.dma_start(out=bqkc_sb[:], in_=bqkc_d[:, :]), 16)
        cstB = cst.n
        b_prep = Sem(nc, es, "b_prep")

        @blk.vector
        def _(v):
            v.wait_ge(barA.h, barA_n)
            v.wait_ge(cst.h, cstB)
            n = b_prep.inc(v.tensor_scalar_mul(bqkc2_sb[:, 0:NH], bqkc_sb[:, 0:NH],
                                               1.0 / 128.0))
            v.wait_ge(b_prep.h, n)
            b_prep.inc(v.tensor_copy(bqkc2_sb[:, NH:32], bqkc_sb[:, NH:32]))
        b_prep_n = b_prep.n

        b_wld = [Sem(nc, es, f"b_wld{j}") for j in range(2)]
        b_mm = Sem(nc, es, "b_mm")
        b_ev = Sem(nc, es, "b_ev")
        b_scrw = [Sem(nc, es, f"b_scrw{j}") for j in range(2)]

        for c in range(32):
            @blk.gpsimd
            def _(g):
                if c == 0:
                    g.wait_ge(barA.h, barA_n)
                    for sp in a_wx + a_wq:
                        g.wait_ge(sp.h, sp.n)
                if c >= 2:
                    g.wait_ge(b_mm.h, marks[("t_b", c - 2, 1)])
                marks[("b_wld", c)] = b_wld[c % 2].inc(
                    g.dma_start(out=wqb[c % 2][:], in_=wq_d[c, :, :]), 16)

            for th in range(2):
                par = (2 * c + th) % 2
                idx = 2 * c + th

                @blk.tensor
                def _(t):
                    if idx == 0:
                        t.wait_ge(barA.h, barA_n)
                    t.wait_ge(b_wld[c % 2].h, marks[("b_wld", c)])
                    if idx >= 2:
                        t.wait_ge(b_ev.h, idx - 1)
                    ins = None
                    for k in range(NK):
                        for nb in range(4):
                            ins = t.matmul(
                                psB[par][:, nb * 512:(nb + 1) * 512],
                                wqb[c % 2][:, k * P:(k + 1) * P],
                                xt_sb[:, k * T + th * D + nb * 512:
                                      k * T + th * D + (nb + 1) * 512],
                                start=(k == 0), stop=(k == NK - 1))
                    marks[("t_b", c, th)] = b_mm.inc(ins)

                @blk.sync
                def _(sy):
                    sy.wait_ge(b_ev.h, idx + 1)
                    dst = (qth_d if c < NH else kth_d)[c % NH, :, th * D:(th + 1) * D]
                    marks[("b_w", idx)] = b_scrw[par].inc(
                        sy.dma_start(out=dst, in_=stg[par][:]), 16)

        @blk.scalar
        def _(s):
            s.wait_ge(b_prep.h, b_prep_n)
            for c in range(32):
                for th in range(2):
                    par = (2 * c + th) % 2
                    idx = 2 * c + th
                    s.wait_ge(b_mm.h, marks[("t_b", c, th)])
                    if idx >= 2:
                        s.wait_ge(b_scrw[par].h, marks[("b_w", idx - 2)])
                    n = b_ev.inc(
                        s.activation(stg[par][:], psB[par][:], AF.Identity,
                                     scale=(1.0 / 128.0 if c < NH else 1.0),
                                     bias=bqkc2_sb[:, c:c + 1]))
                    assert n == idx + 1

        @blk.sync
        def _(sy):
            for sp in b_scrw:
                sy.wait_ge(sp.h, sp.n)
        barB_n = barrier(barB, ("tensor", "scalar", "sync", "gpsimd"))
        esB.close()
    esA.close()

    barV = Sem(nc, es, "barV")
    barV_n = 0
    if "V" in phases:
        esV = ExitStack()
        wft_sb = esV.enter_context(nc.sbuf_tensor("wft_sb", [P, NK * D], BF16))
        wvr_sb = esV.enter_context(nc.sbuf_tensor("wvr_sb", [P, NK * D], BF16))
        ws_bc = esV.enter_context(nc.sbuf_tensor("ws_bc", [P, D], F32))
        fb_bc = esV.enter_context(nc.sbuf_tensor("fb_bc", [P, D], F32))
        xts = [esV.enter_context(nc.sbuf_tensor(f"xts{j}", [P, D], BF16)) for j in range(2)]
        qts = [esV.enter_context(nc.sbuf_tensor(f"qts{j}", [P, D], BF16)) for j in range(2)]
        vst = [esV.enter_context(nc.sbuf_tensor(f"vst{j}", [P, D], BF16)) for j in range(2)]
        ff1 = esV.enter_context(nc.sbuf_tensor("ff1", [P, D], F32))
        gg16 = [esV.enter_context(nc.sbuf_tensor(f"gg16_{j}", [P, D], F16)) for j in range(2)]
        psF = esV.enter_context(nc.psum_tensor("psF", [P, D], F32))
        psV = esV.enter_context(nc.psum_tensor("psV", [P, D], F32))
        vec1_sb = esV.enter_context(nc.sbuf_tensor("vec1_sb", [1, D], F32))

        v_wld = Sem(nc, es, "v_wld")
        v_lx = [Sem(nc, es, f"v_lx{j}") for j in range(2)]
        v_lq = [Sem(nc, es, f"v_lq{j}") for j in range(2)]
        v_mm = Sem(nc, es, "v_mm")
        v_evv = Sem(nc, es, "v_evv")
        v_evs = Sem(nc, es, "v_evs")
        g_gw = [Sem(nc, es, f"g_gw{j}") for j in range(2)]
        g_vw = [Sem(nc, es, f"g_vw{j}") for j in range(2)]

        @blk.sync
        def _(sy):
            sy.wait_ge(barB.h, barB_n)
            cst.inc(sy.dma_start(out=vec1_sb[:], in_=wsc_d[:, :]), 16)
        cstV = cst.n

        @blk.gpsimd
        def _(g):
            g.wait_ge(barB.h, barB_n)
            for k in range(NK):
                v_wld.inc(g.dma_start(out=wft_sb[:, k * D:(k + 1) * D],
                                      in_=wf_d[k * P:(k + 1) * P, :]), 16)
            for k in range(NK):
                v_wld.inc(g.dma_start(out=wvr_sb[:, k * D:(k + 1) * D],
                                      in_=wv_d[k, :, :]), 16)
        v_wld_done = v_wld.n

        @blk.tensor
        def _(t):
            t.wait_ge(barB.h, barB_n)
            t.wait_ge(cst.h, cstV)
            for nb in range(4):
                ins = t.matmul(psF[:, nb * 512:(nb + 1) * 512],
                               onr_sb[0:1, 0:P], vec1_sb[0:1, nb * 512:(nb + 1) * 512],
                               start=True, stop=True)
            marks[("v_bc1",)] = v_mm.inc(ins)

        @blk.vector
        def _(v):
            v.wait_ge(v_mm.h, marks[("v_bc1",)])
            marks[("v_bc1c",)] = v_evv.inc(v.tensor_copy(ws_bc[:], psF[:]))

        @blk.sync
        def _(sy):
            sy.wait_ge(v_mm.h, marks[("v_bc1",)])
            cst.inc(sy.dma_start(out=vec1_sb[:], in_=fbi_d[:, :]), 16)
        cstV2 = cst.n

        @blk.tensor
        def _(t):
            t.wait_ge(v_evv.h, marks[("v_bc1c",)])
            t.wait_ge(cst.h, cstV2)
            for nb in range(4):
                ins = t.matmul(psF[:, nb * 512:(nb + 1) * 512],
                               onr_sb[0:1, 0:P], vec1_sb[0:1, nb * 512:(nb + 1) * 512],
                               start=True, stop=True)
            marks[("v_bc2",)] = v_mm.inc(ins)

        @blk.vector
        def _(v):
            v.wait_ge(v_mm.h, marks[("v_bc2",)])
            marks[("v_bc2c",)] = v_evv.inc(v.tensor_copy(fb_bc[:], psF[:]))
        ffn_psF_ready = marks[("v_bc2c",)]

        for i in range(NT):
            b = i % 2

            @blk.sync
            def _(sy):
                if i == 0:
                    sy.wait_ge(barB.h, barB_n)
                if i >= 2:
                    sy.wait_ge(v_mm.h, marks[("t_v", i - 2)])
                marks[("v_lx", i)] = v_lx[b].inc(
                    sy.dma_start(out=xts[b][:], in_=xt2_d[i, :, :]), 16)
                if i >= 2:
                    sy.wait_ge(v_mm.h, marks[("t_f", i - 2)])
                marks[("v_lq", i)] = v_lq[b].inc(
                    sy.dma_start(out=qts[b][:], in_=qt2_d[i, :, :]), 16)

            @blk.tensor
            def _(t):
                t.wait_ge(v_lq[b].h, marks[("v_lq", i)])
                if i == 0:
                    t.wait_ge(v_evv.h, ffn_psF_ready)
                else:
                    t.wait_ge(v_evv.h, marks[("v_f1", i - 1)])
                ins = None
                for k in range(NK):
                    if i == 0:
                        t.wait_ge(v_wld.h, 16 * (k + 1))
                    for nb in range(4):
                        ins = t.matmul(psF[:, nb * 512:(nb + 1) * 512],
                                       qts[b][:, k * P:(k + 1) * P],
                                       wft_sb[:, k * D + nb * 512:k * D + (nb + 1) * 512],
                                       start=(k == 0), stop=(k == NK - 1))
                marks[("t_f", i)] = v_mm.inc(ins)
                t.wait_ge(v_lx[b].h, marks[("v_lx", i)])
                if i >= 1:
                    t.wait_ge(v_evs.h, 3 * i)
                for k in range(NK):
                    if i == 0:
                        t.wait_ge(v_wld.h, 16 * (NK + k + 1))
                    for nb in range(4):
                        ins = t.matmul(psV[:, nb * 512:(nb + 1) * 512],
                                       xts[b][:, k * P:(k + 1) * P],
                                       wvr_sb[:, k * D + nb * 512:k * D + (nb + 1) * 512],
                                       start=(k == 0), stop=(k == NK - 1))
                marks[("t_v", i)] = v_mm.inc(ins)

            @blk.vector
            def _(v):
                v.wait_ge(v_mm.h, marks[("t_f", i)])
                if i >= 1:
                    v.wait_ge(v_evs.h, 3 * i - 2)
                marks[("v_f1", i)] = v_evv.inc(
                    v.tensor_scalar(ff1[:], in0=psF[:], scalar1=is_sb[:, i:i + 1],
                                    scalar2=None, op0=ALU.mult))
                v.wait_ge(v_evv.h, marks[("v_f1", i)])
                n = v_evv.inc(v.tensor_tensor(ff1[:], ff1[:], ws_bc[:], op=ALU.mult))
                v.wait_ge(v_evv.h, n)
                marks[("v_f3", i)] = v_evv.inc(
                    v.tensor_tensor(ff1[:], ff1[:], fb_bc[:], op=ALU.add))

            @blk.gpsimd
            def _(g):
                g.wait_ge(v_evs.h, 3 * i + 2)
                marks[("g_gw", i)] = g_gw[b].inc(
                    g.dma_start(out=g2_d[i, :, :], in_=gg16[b][:]), 16)
                g.wait_ge(v_evs.h, 3 * i + 3)
                marks[("g_vw", i)] = g_vw[b].inc(
                    g.dma_start(out=vn2_d[i, :, :], in_=vst[b][:]), 16)

        @blk.scalar
        def _(s):
            for i in range(NT):
                b = i % 2
                s.wait_ge(v_evv.h, marks[("v_f3", i)])
                if i >= 2:
                    s.wait_ge(g_gw[b].h, marks[("g_gw", i - 2)])
                n = v_evs.inc(s.copy(gg16[b][:], ff1[:]))
                s.wait_ge(v_evs.h, n)
                n = v_evs.inc(s.activation(gg16[b][:], gg16[b][:], AF.Silu))
                assert n == 3 * i + 2
                s.wait_ge(v_mm.h, marks[("t_v", i)])
                if i >= 2:
                    s.wait_ge(g_vw[b].h, marks[("g_vw", i - 2)])
                n = v_evs.inc(s.copy(vst[b][:], psV[:]))
                assert n == 3 * i + 3

        @blk.gpsimd
        def _(g):
            for sp in g_gw + g_vw:
                g.wait_ge(sp.h, sp.n)
        barV_n = barrier(barV, ("tensor", "scalar", "vector", "gpsimd", "sync"))
        esV.close()

    barC = Sem(nc, es, "barC")
    barC_n = 0
    c_yw = [Sem(nc, es, f"c_yw{j}") for j in range(2)]
    if "C" in phases:
        esC = ExitStack()
        kt_sb = [esC.enter_context(nc.sbuf_tensor(f"kt{j}", [P, T], BF16)) for j in range(3)]
        vt_sb = [esC.enter_context(nc.sbuf_tensor(f"vt{j}", [P, T], BF16)) for j in range(3)]
        qg_sb = [esC.enter_context(nc.sbuf_tensor(f"qg{j}", [P, NH * 512], BF16)) for j in range(2)]
        ytg_sb = [esC.enter_context(nc.sbuf_tensor(f"ytg{j}", [P, NH * 512], BF16)) for j in range(2)]
        pt_sb = [esC.enter_context(nc.sbuf_tensor(f"pt{j}", [P, NT * 512], BF16)) for j in range(2)]
        rl_sb = esC.enter_context(nc.sbuf_tensor("rl_sb", [1, 512], F32))
        bc_sb = esC.enter_context(nc.sbuf_tensor("bc_sb", [P, 512], F32))
        psS = [esC.enter_context(nc.psum_tensor(f"psS{j}", [P, 1024], F32)) for j in range(2)]
        psY = [esC.enter_context(nc.psum_tensor(f"psY{j}", [P, 512], F32)) for j in range(2)]
        psDB = [esC.enter_context(nc.psum_tensor(f"psDB{j}", [P, 512], F32)) for j in range(2)]

        c_kld = [Sem(nc, es, f"c_kld{j}") for j in range(3)]
        c_vld = [Sem(nc, es, f"c_vld{j}") for j in range(3)]
        c_qld = [Sem(nc, es, f"c_qld{j}") for j in range(2)]
        c_mm = Sem(nc, es, "c_mm")
        c_ex = Sem(nc, es, "c_ex")
        c_ve = Sem(nc, es, "c_ve")

        steps = [(g8, h) for g8 in range(NG) for h in range(NH)]
        NP = len(steps)
        gbase = {}
        gctr = [0]

        def jmax_of(pos):
            return GQ * steps[pos][0] + GQ

        def ntrip_of(pos):
            return (jmax_of(pos) + 1) // 2

        def emit_loads(pos):
            g8, h = steps[pos]
            jmax = jmax_of(pos)
            b = pos % 3

            @blk.sync
            def _(sy):
                if pos == 0:
                    sy.wait_ge(barV.h, barV_n)
                if pos >= 3:
                    sy.wait_ge(c_mm.h, max(marks[("c_dn", pos - 3)],
                                           marks[("c_pv", pos - 3)]))
                marks[("c_k", pos)] = c_kld[b].inc(
                    sy.dma_start(out=kt_sb[b][:, 0:jmax * P],
                                 in_=kth_d[h, :, 0:jmax * P]), 16)
                qload = [0] if pos == 0 else (
                    [g8 + 1] if (h == 2 and g8 + 1 < NG) else [])
                for gq in qload:
                    gb2 = gq % 2
                    if gq >= 2:
                        sy.wait_ge(c_mm.h, max(marks[("c_dn", (gq - 1) * NH - 1)],
                                               marks[("c_pv", (gq - 1) * NH - 1)]))
                    srcq = qth_d[:, :, gq * 512:(gq + 1) * 512].rearrange("h p t -> p h t")
                    dstq = qg_sb[gb2][:].rearrange("p (h t) -> p h t", h=NH)
                    marks[("c_q", gq)] = c_qld[gb2].inc(
                        sy.dma_start(out=dstq, in_=srcq), 16)

            @blk.gpsimd
            def _(g):
                if pos == 0:
                    g.wait_ge(barV.h, barV_n)
                if pos >= 3:
                    g.wait_ge(c_mm.h, max(marks[("c_dn", pos - 3)],
                                          marks[("c_pv", pos - 3)]))
                srcv = vn2_d[0:jmax, :, h * P:(h + 1) * P].rearrange("j p d -> p j d")
                dstv = vt_sb[b][:, 0:jmax * P].rearrange("p (j d) -> p j d", j=jmax)
                marks[("c_v", pos)] = c_vld[b].inc(g.dma_start(out=dstv, in_=srcv), 16)

        def emit_S_triple(t, pos, tr):
            g8, h = steps[pos]
            jmax = jmax_of(pos)
            b = pos % 3
            gb = g8 % 2
            g = gbase[pos] + tr
            if g >= 2:
                t.wait_ge(c_ex.h, g - 1)
            ins = None
            for jj in range(2):
                j = tr * 2 + jj
                if j >= jmax:
                    break
                ins = t.matmul(psS[g % 2][:, jj * 512:(jj + 1) * 512],
                               kt_sb[b][:, j * P:(j + 1) * P],
                               qg_sb[gb][:, h * 512:(h + 1) * 512],
                               start=True, stop=True)
            marks[("c_st", pos, tr)] = c_mm.inc(ins)

        def emit_jrange(t, pos, jlo, jhi, jmaxp, which):
            g8, h = steps[pos]
            b = pos % 3
            diag = GQ * g8
            t.wait_ge(c_ex.h, gbase[pos] + (jhi - 1) // 2 + 1)
            if jlo == 0:
                t.wait_ge(c_vld[pos % 3].h, marks[("c_v", pos)])
                if pos >= 2:
                    t.wait_ge(c_ve.h, marks[("c_ye", pos - 2)])
            if jlo <= diag < jhi:
                t.wait_ge(c_ve.h, marks[("c_msk", pos)])
            ins = None
            for j in range(jlo, jhi):
                s0 = max(0, j - diag) * P
                if which == "pv":
                    ins = t.matmul(psY[pos % 2][:, s0:512],
                                   vt_sb[b][:, j * P:(j + 1) * P],
                                   pt_sb[pos % 2][:, j * 512 + s0:(j + 1) * 512],
                                   start=(j == 0), stop=(j == jmaxp - 1))
                else:
                    ins = t.matmul(psDB[pos % 2][:, s0:512], onm_sb[:],
                                   pt_sb[pos % 2][:, j * 512 + s0:(j + 1) * 512],
                                   start=(j == 0), stop=(j == jmaxp - 1))
            if jhi == jmaxp:
                marks[("c_pv" if which == "pv" else "c_dn", pos)] = c_mm.inc(ins)

        def emit_step(pos):
            g8, h = steps[pos]
            b = pos % 3
            ntrip = ntrip_of(pos)
            gbase[pos] = gctr[0]
            gctr[0] += ntrip
            jmax1 = jmax_of(pos - 1) if pos >= 1 else 0

            @blk.tensor
            def _(t):
                if pos == 0:
                    t.wait_ge(barV.h, barV_n)
                t.wait_ge(c_kld[b].h, marks[("c_k", pos)])
                t.wait_ge(c_qld[g8 % 2].h, marks[("c_q", g8)])
                H = max(1, (ntrip + 1) // 2)
                for tr in range(ntrip):
                    if pos >= 1:
                        if tr < H:
                            dlo = tr * jmax1 // H
                            dhi = (tr + 1) * jmax1 // H
                            if dlo < dhi:
                                emit_jrange(t, pos - 1, dlo, dhi, jmax1, "dn")
                        jlo = tr * jmax1 // ntrip
                        jhi = (tr + 1) * jmax1 // ntrip
                        if jlo < jhi:
                            emit_jrange(t, pos - 1, jlo, jhi, jmax1, "pv")
                    emit_S_triple(t, pos, tr)

            if pos >= 1:
                @blk.vector
                def _(v):
                    v.wait_ge(c_mm.h, marks[("c_dn", pos - 1)])
                    marks[("c_rc", pos - 1)] = c_ve.inc(
                        v.reciprocal(rl_sb[:], psDB[(pos - 1) % 2][0:1, :]))

        def emit_final_pvdn(pos):
            jmax1 = jmax_of(pos)

            @blk.tensor
            def _(t):
                emit_jrange(t, pos, 0, jmax1, jmax1, "dn")
                emit_jrange(t, pos, 0, jmax1, jmax1, "pv")

            @blk.vector
            def _(v):
                v.wait_ge(c_mm.h, marks[("c_dn", pos)])
                marks[("c_rc", pos)] = c_ve.inc(
                    v.reciprocal(rl_sb[:], psDB[pos % 2][0:1, :]))

        def emit_mask(pos):
            g8, h = steps[pos]

            @blk.vector
            def _(v):
                ins = None
                for sub in range(GQ):
                    j = GQ * g8 + sub
                    v.wait_ge(c_ex.h, gbase[pos] + j // 2 + 1)
                    sl = pt_sb[pos % 2][:, j * 512 + sub * P: j * 512 + (sub + 1) * P]
                    ins = v.tensor_tensor(sl, sl, tri_sb[:], op=ALU.mult)
                marks[("c_msk", pos)] = c_ve.inc(ins)

        def emit_BC(pos):
            @blk.tensor
            def _(t):
                t.wait_ge(c_ve.h, marks[("c_rc", pos)])
                marks[("c_bc", pos)] = c_mm.inc(
                    t.matmul(psDB[pos % 2][:, 0:512], onr_sb[0:1, 0:P], rl_sb[0:1, :],
                             start=True, stop=True))

        def emit_evac(pos):
            g8, h = steps[pos]
            gb = g8 % 2

            @blk.vector
            def _(v):
                v.wait_ge(c_mm.h, marks[("c_bc", pos)])
                n = c_ve.inc(v.tensor_copy(bc_sb[:], psDB[pos % 2][:]))
                v.wait_ge(c_mm.h, marks[("c_pv", pos)])
                if h == 0 and g8 >= 2:
                    v.wait_ge(c_yw[g8 % 2].h, marks[("c_ytw", g8 - 2)])
                v.wait_ge(c_ve.h, n)
                n = c_ve.inc(
                    v.tensor_tensor(ytg_sb[gb][:, h * 512:(h + 1) * 512],
                                    psY[pos % 2][:], bc_sb[:], op=ALU.mult))
                v.wait_ge(c_ve.h, n)
                marks[("c_ye", pos)] = c_ve.inc(
                    v.tensor_scalar_add(ytg_sb[gb][:, h * 512:(h + 1) * 512],
                                        ytg_sb[gb][:, h * 512:(h + 1) * 512],
                                        bqvc_sb[:, h:h + 1]))

            if h == NH - 1:
                @blk.sync
                def _(sy):
                    sy.wait_ge(c_ve.h, marks[("c_ye", pos)])
                    dstv = yt_d[:, :, g8 * 512:(g8 + 1) * 512].rearrange("h p t -> p h t")
                    srcv = ytg_sb[gb][:].rearrange("p (h t) -> p h t", h=NH)
                    marks[("c_ytw", g8)] = c_yw[gb].inc(sy.dma_start(out=dstv, in_=srcv), 16)

        def emit_exp_all():
            @blk.scalar
            def _(s):
                for pos in range(NP):
                    jmax = jmax_of(pos)
                    ntrip = ntrip_of(pos)
                    for tr in range(ntrip):
                        g = gbase[pos] + tr
                        nj = min(2, jmax - tr * 2)
                        s.wait_ge(c_mm.h, marks[("c_st", pos, tr)])
                        assert c_ex.n == g, "exp order mismatch"
                        c_ex.inc(s.activation(
                            pt_sb[pos % 2][:, tr * 1024:tr * 1024 + nj * 512],
                            psS[g % 2][:, 0:nj * 512], AF.Exp))

        emit_loads(0)
        for pos in range(NP):
            if pos >= 2:
                emit_BC(pos - 2)
                emit_evac(pos - 2)
            emit_step(pos)
            emit_mask(pos)
            if pos + 1 < NP:
                emit_loads(pos + 1)
        emit_BC(NP - 2)
        emit_evac(NP - 2)
        emit_final_pvdn(NP - 1)
        emit_BC(NP - 1)
        emit_evac(NP - 1)
        emit_exp_all()

        @blk.sync
        def _(sy):
            for sp in c_yw:
                sy.wait_ge(sp.h, sp.n)
        barC_n = barrier(barC, ("tensor", "scalar", "vector", "gpsimd", "sync"))
        esC.close()

    if "F" in phases:
        esF = ExitStack()
        wo_sb = esF.enter_context(nc.sbuf_tensor("wo_sb", [P, NK * D], BF16))
        ytb = [esF.enter_context(nc.sbuf_tensor(f"ytb{j}", [P, D], BF16)) for j in range(2)]
        gg2 = [esF.enter_context(nc.sbuf_tensor(f"gg2_{j}", [P, D], F16)) for j in range(2)]
        gfb = esF.enter_context(nc.sbuf_tensor("gfb", [P, D], F32))
        xf2 = [esF.enter_context(nc.sbuf_tensor(f"xf2_{j}", [P, D], F32)) for j in range(2)]
        outb = [esF.enter_context(nc.sbuf_tensor(f"outb{j}", [P, D], F32)) for j in range(2)]
        psO = [esF.enter_context(nc.psum_tensor(f"psO{j}", [P, D], F32)) for j in range(2)]
        bout_sb = esF.enter_context(nc.sbuf_tensor("bout_sb", [1, D], F32))
        bob_bc = esF.enter_context(nc.sbuf_tensor("bob_bc", [P, D], F32))

        @blk.sync
        def _(sy):
            sy.wait_ge(barC.h, barC_n)
            cst.inc(sy.dma_start(out=bout_sb[:], in_=bout_d[:, :]), 16)
        cstF = cst.n
        f_prep = Sem(nc, es, "f_prep")
        f_prepv = Sem(nc, es, "f_prepv")

        @blk.tensor
        def _(t):
            t.wait_ge(barC.h, barC_n)
            t.wait_ge(cst.h, cstF)
            for nb in range(4):
                ins = t.matmul(psO[0][:, nb * 512:(nb + 1) * 512], oni_sb[:],
                               bout_sb[0:1, nb * 512:(nb + 1) * 512],
                               start=True, stop=True)
            f_prep.inc(ins)
        f_prep_n = f_prep.n

        @blk.vector
        def _(v):
            v.wait_ge(f_prep.h, f_prep_n)
            f_prepv.inc(v.tensor_copy(bob_bc[:], psO[0][:]))
        f_prepv_n = f_prepv.n

        f_wo = Sem(nc, es, "f_wo")
        f_x = [Sem(nc, es, f"f_x{j}") for j in range(2)]
        f_g = [Sem(nc, es, f"f_g{j}") for j in range(2)]
        f_y = [Sem(nc, es, f"f_y{j}") for j in range(2)]
        f_mm = Sem(nc, es, "f_mm")
        f_cb = Sem(nc, es, "f_cb")
        f_out = [Sem(nc, es, f"f_out{j}") for j in range(2)]

        @blk.gpsimd
        def _(g):
            g.wait_ge(barC.h, barC_n)
            for sp in c_yw:
                g.wait_ge(sp.h, sp.n)
            for k in range(NK):
                f_wo.inc(g.dma_start(out=wo_sb[:, k * D:(k + 1) * D],
                                     in_=wo_d[k * P:(k + 1) * P, :]), 16)
        f_wo_done = f_wo.n

        for i in range(NT):
            b = i % 2

            @blk.sync
            def _(sy):
                if i == 0:
                    sy.wait_ge(barC.h, barC_n)
                if i >= 2:
                    sy.wait_ge(f_cb.h, marks[("f_cb", i - 2)])
                marks[("f_x", i)] = f_x[b].inc(
                    sy.dma_start(out=xf2[b][:], in_=x_d[i * P:(i + 1) * P, :]), 16)
                marks[("f_g", i)] = f_g[b].inc(
                    sy.dma_start(out=gg2[b][:], in_=g2_d[i, :, :]), 16)

            @blk.gpsimd
            def _(g):
                if i >= 2:
                    g.wait_ge(f_mm.h, marks[("f_mm", i - 2)])
                src = yt_d[:, :, i * P:(i + 1) * P].rearrange("h p t -> p h t")
                dstv = ytb[b][:].rearrange("p (h t) -> p h t", h=NH)
                marks[("f_y", i)] = f_y[b].inc(g.dma_start(out=dstv, in_=src), 16)

            @blk.tensor
            def _(t):
                t.wait_ge(f_wo.h, f_wo_done)
                if i == 0:
                    t.wait_ge(f_prepv.h, f_prepv_n)
                t.wait_ge(f_y[b].h, marks[("f_y", i)])
                if i >= 2:
                    t.wait_ge(f_cb.h, marks[("f_cb", i - 2)])
                for nb in range(4):
                    t.matmul(psO[b][:, nb * 512:(nb + 1) * 512], onmf_sb[:],
                             bob_bc[:, nb * 512:(nb + 1) * 512],
                             start=True, stop=False)
                ins = None
                for hh in range(NH):
                    for nb in range(4):
                        ins = t.matmul(psO[b][:, nb * 512:(nb + 1) * 512],
                                       ytb[b][:, hh * P:(hh + 1) * P],
                                       wo_sb[:, hh * D + nb * 512:hh * D + (nb + 1) * 512],
                                       start=False, stop=(hh == NH - 1))
                marks[("f_mm", i)] = f_mm.inc(ins)

            @blk.vector
            def _(v):
                v.wait_ge(f_g[b].h, marks[("f_g", i)])
                n = f_cb.inc(v.tensor_copy(gfb[:], gg2[b][:]))
                v.wait_ge(f_mm.h, marks[("f_mm", i)])
                if i >= 2:
                    v.wait_ge(f_out[b].h, marks[("f_o", i - 2)])
                v.wait_ge(f_cb.h, n)
                n = f_cb.inc(v.tensor_tensor(outb[b][:], psO[b][:], gfb[:], op=ALU.mult))
                v.wait_ge(f_x[b].h, marks[("f_x", i)])
                v.wait_ge(f_cb.h, n)
                marks[("f_cb", i)] = f_cb.inc(
                    v.tensor_tensor(outb[b][:], outb[b][:], xf2[b][:], op=ALU.add))

            @blk.sync
            def _(sy):
                sy.wait_ge(f_cb.h, marks[("f_cb", i)])
                marks[("f_o", i)] = f_out[b].inc(
                    sy.dma_start(out=out_d[i * P:(i + 1) * P, :], in_=outb[b][:]), 16)

        @blk.sync
        def _(sy):
            for sp in f_out:
                sy.wait_ge(sp.h, sp.n)
        esF.close()

    es.close()
    return nc


def _host_inputs(L0_attn_x, L0_attn_W_qkv, L0_attn_b_qkv, L0_attn_W_out,
                 L0_attn_b_out, L0_ffn_weight, L0_ffn_w_scale, L0_ffn_bias):
    bf16 = ml_dtypes.bfloat16
    Wq = np.asarray(L0_attn_W_qkv, dtype=np.float32)
    wq = np.ascontiguousarray(
        Wq[:, :2 * D].reshape(NK, P, 32, P).transpose(2, 1, 0, 3))
    wv = np.ascontiguousarray(Wq[:, 2 * D:].reshape(NK, P, D))
    wo = np.ascontiguousarray(np.asarray(L0_attn_W_out, dtype=np.float32))
    wf = np.ascontiguousarray(np.asarray(L0_ffn_weight).T)
    bq3 = np.asarray(L0_attn_b_qkv, np.float32)
    bqkc = np.ascontiguousarray(bq3[:2 * D].reshape(32, P).T)
    bqvc = np.ascontiguousarray(bq3[2 * D:].reshape(NH, P).T)
    bout = np.ascontiguousarray(np.asarray(L0_attn_b_out, np.float32).reshape(1, D))
    wsc = np.ascontiguousarray(np.asarray(L0_ffn_w_scale, np.float32).reshape(-1)[None, :])
    fbi = np.ascontiguousarray(np.asarray(L0_ffn_bias, np.float32).reshape(1, D))
    idn = np.eye(P, dtype=bf16)
    tri = np.triu(np.ones((P, P), np.float32)).astype(bf16)
    onc = np.ones((P, 1), bf16)
    onm = np.ones((P, P), bf16)
    oni = np.full((1, P), 1.0 / 128.0, np.float32)
    onmf = np.ones((P, P), np.float32)
    onr = np.ones((1, 512), np.float32)
    shared = dict(wq=wq, wv=wv, wo=wo, wf=wf, bqkc=bqkc, bqvc=bqvc, bout=bout,
                  wsc=wsc, fbi=fbi, idn=idn, tri=tri, onc=onc, onm=onm, onr=onr,
                  oni=oni, onmf=onmf)
    xs = np.asarray(L0_attn_x, dtype=np.float32)
    maps = []
    for b in range(8):
        m = dict(shared)
        m["x"] = np.ascontiguousarray(xs[b])
        maps.append(m)
    return maps


def kernel(L0_attn_x, L0_attn_W_qkv, L0_attn_b_qkv, L0_attn_W_out,
           L0_attn_b_out, L0_ffn_weight, L0_ffn_w_scale, L0_ffn_bias):
    maps = _host_inputs(L0_attn_x, L0_attn_W_qkv, L0_attn_b_qkv, L0_attn_W_out,
                        L0_attn_b_out, L0_ffn_weight, L0_ffn_w_scale, L0_ffn_bias)
    nc = build_program()
    res = run_bass_kernel_spmd(nc, maps, core_ids=list(range(8)))
    return np.stack([np.asarray(r["out"]) for r in res.results], axis=0)


# version 42
# speedup vs baseline: 1.4558x; 1.4558x over previous
import numpy as np
import ml_dtypes
import concourse.bass as bass
import concourse.mybir as mybir
from concourse.bass_utils import run_bass_kernel_spmd
from contextlib import ExitStack

F32 = mybir.dt.float32
BF16 = mybir.dt.bfloat16
F16 = mybir.dt.float16
I32 = mybir.dt.int32
I8 = mybir.dt.int8
AF = mybir.ActivationFunctionType
ALU = mybir.AluOpType
AX = mybir.AxisListType

P = 128
NT = 32
NK = 16
NH = 16
D = 2048
T = 4096
GQ = 4
NG = NT // GQ

DEBUG_SCRATCH = False
PHASES = ("A", "B", "V", "C", "F")


class Sem:

    def __init__(self, nc, es, name):
        self.h = es.enter_context(nc.semaphore(name))
        self.n = 0

    def inc(self, instr, k=1):
        instr.then_inc(self.h, k)
        self.n += k
        return self.n


def build_program(debug=DEBUG_SCRATCH, phases=PHASES):
    nc = bass.Bass("TRN2", target_bir_lowering=False)
    scr_kind = "ExternalOutput" if debug else "Internal"

    x_d = nc.dram_tensor("x", [T, D], F32, kind="ExternalInput").ap()
    wq_d = nc.dram_tensor("wq", [32, P, D], F32, kind="ExternalInput").ap()
    wv_d = nc.dram_tensor("wv", [NK, P, D], F32, kind="ExternalInput").ap()
    wo_d = nc.dram_tensor("wo", [D, D], F32, kind="ExternalInput").ap()
    wf_d = nc.dram_tensor("wf", [D, D], I8, kind="ExternalInput").ap()
    bqkc_d = nc.dram_tensor("bqkc", [P, 32], F32, kind="ExternalInput").ap()
    bqvc_d = nc.dram_tensor("bqvc", [P, NH], F32, kind="ExternalInput").ap()
    bout_d = nc.dram_tensor("bout", [1, D], F32, kind="ExternalInput").ap()
    wsc_d = nc.dram_tensor("wsc", [1, D], F32, kind="ExternalInput").ap()
    fbi_d = nc.dram_tensor("fbi", [1, D], F32, kind="ExternalInput").ap()
    idn_d = nc.dram_tensor("idn", [P, P], BF16, kind="ExternalInput").ap()
    tri_d = nc.dram_tensor("tri", [P, P], BF16, kind="ExternalInput").ap()
    onc_d = nc.dram_tensor("onc", [P, 1], BF16, kind="ExternalInput").ap()
    onm_d = nc.dram_tensor("onm", [P, P], BF16, kind="ExternalInput").ap()
    onr_d = nc.dram_tensor("onr", [1, 512], F32, kind="ExternalInput").ap()
    oni_d = nc.dram_tensor("oni", [1, P], F32, kind="ExternalInput").ap()
    onmf_d = nc.dram_tensor("onmf", [P, P], F32, kind="ExternalInput").ap()
    out_d = nc.dram_tensor("out", [T, D], F32, kind="ExternalOutput").ap()

    xt2_d = nc.dram_tensor("xt2", [NT, P, NK, P], BF16, kind=scr_kind).ap()
    qt2_d = nc.dram_tensor("qt2", [NT, P, D], BF16, kind=scr_kind).ap()
    qth_d = nc.dram_tensor("qth", [NH, P, T], BF16, kind=scr_kind).ap()
    kth_d = nc.dram_tensor("kth", [NH, P, T], BF16, kind=scr_kind).ap()
    vn2_d = nc.dram_tensor("vn2", [NT, P, D], BF16, kind=scr_kind).ap()
    g2_d = nc.dram_tensor("g2", [NT, P, D], F16, kind=scr_kind).ap()
    yt_d = nc.dram_tensor("yt", [NH, P, T], BF16, kind=scr_kind).ap()
    is_d = nc.dram_tensor("isd", [P, NT], F32, kind=scr_kind).ap()

    es = ExitStack()
    idn_sb = es.enter_context(nc.sbuf_tensor("idn_sb", [P, P], BF16))
    tri_sb = es.enter_context(nc.sbuf_tensor("tri_sb", [P, P], BF16))
    onc_sb = es.enter_context(nc.sbuf_tensor("onc_sb", [P, 1], BF16))
    onm_sb = es.enter_context(nc.sbuf_tensor("onm_sb", [P, P], BF16))
    onr_sb = es.enter_context(nc.sbuf_tensor("onr_sb", [1, 512], F32))
    oni_sb = es.enter_context(nc.sbuf_tensor("oni_sb", [1, P], F32))
    bqvc_sb = es.enter_context(nc.sbuf_tensor("bqvc_sb", [P, NH], F32))
    is_sb = es.enter_context(nc.sbuf_tensor("is_sb", [P, NT], F32))

    blk = es.enter_context(nc.Block())
    marks = {}
    cst = Sem(nc, es, "cst")

    @blk.sync
    def _(sy):
        for dst, srcd in ((idn_sb, idn_d), (tri_sb, tri_d), (onc_sb, onc_d),
                          (onm_sb, onm_d), (onr_sb, onr_d), (oni_sb, oni_d),
                          (bqvc_sb, bqvc_d)):
            cst.inc(sy.dma_start(out=dst[:], in_=srcd[:, :]), 16)
    cst_done = cst.n

    def barrier(sem, engines):
        for engname in engines:
            @getattr(blk, engname)
            def _(eng):
                sem.inc(eng.nop())
        return sem.n

    esA = ExitStack()
    xt_sb = esA.enter_context(nc.sbuf_tensor("xt_sb", [P, NK * T], BF16))
    barA = Sem(nc, es, "barA")
    a_wx = [Sem(nc, es, f"a_wx{j}") for j in range(2)]
    a_wq = [Sem(nc, es, f"a_wq{j}") for j in range(2)]
    barA_n = 0

    if "A" in phases:
        esA2 = ExitStack()
        xf = [esA2.enter_context(nc.sbuf_tensor(f"xf{j}", [P, D], F32)) for j in range(3)]
        xnf = [esA2.enter_context(nc.sbuf_tensor(f"xnf{j}", [P, D], F32)) for j in range(2)]
        qi8 = [esA2.enter_context(nc.sbuf_tensor(f"qi8_{j}", [P, D], I8)) for j in range(2)]
        xnbt = [esA2.enter_context(nc.sbuf_tensor(f"xnbt{j}", [P, D], BF16)) for j in range(2)]
        qbt = [esA2.enter_context(nc.sbuf_tensor(f"qbt{j}", [P, D], BF16)) for j in range(3)]
        qT_st = [esA2.enter_context(nc.sbuf_tensor(f"qT{j}", [P, D], BF16)) for j in range(2)]
        bn6 = [esA2.enter_context(nc.sbuf_tensor(f"bn6_{j}", [P, 24], F32)) for j in range(3)]
        stt = [[esA2.enter_context(nc.sbuf_tensor(f"st{j}_{k}", [P, 1], F32))
                for k in range(6)] for j in range(3)]
        agg = [esA2.enter_context(nc.sbuf_tensor(f"agg{j}", [P, 2], F32)) for j in range(3)]
        psA = [esA2.enter_context(nc.psum_tensor(f"psA{j}", [P, 1024], BF16)) for j in range(4)]
        psQ = [esA2.enter_context(nc.psum_tensor(f"psQ{j}", [P, 1024], BF16)) for j in range(4)]

        a_xld = [Sem(nc, es, f"a_xld{j}") for j in range(3)]
        a_s = Sem(nc, es, "a_s")
        a_v = Sem(nc, es, "a_v")
        a_t = Sem(nc, es, "a_t")
        a_g = Sem(nc, es, "a_g")
        VARP, RSTD, NMRS, AM = range(4)
        xt_view = xt_sb[:].rearrange("p (k t) -> p k t", k=NK)

        def sA(i, k):
            if i == 0:
                return k - 1 if k == 7 else k
            return 6 + 7 * (i - 1) + k

        def sEvq0(j):
            if j == NT - 1:
                return sA(NT - 1, 7) + 1
            return sA(j + 1, 6)

        def emit_tail(j):
            bj = j % 2

            @blk.vector
            def _(v):
                st = stt[j % 3]
                v.wait_ge(a_s.h, sA(j, 4))
                n = a_v.inc(v.reduce_max(st[AM][:], xnf[bj][:], axis=AX.X,
                                         apply_absolute_value=True))
                v.wait_ge(a_v.h, n)
                n = a_v.inc(v.tensor_scalar(is_sb[:, j:j + 1], in0=st[AM][:],
                                            scalar1=1e-7, scalar2=1.0 / 127.0,
                                            op0=ALU.max, op1=ALU.mult))
                v.wait_ge(a_v.h, n)
                marks[("v_rsc", j)] = a_v.inc(
                    v.reciprocal(st[AM][:], is_sb[:, j:j + 1]))

            @blk.tensor
            def _(t):
                if j == 0:
                    t.wait_ge(cst.h, cst_done)
                for half in range(2):
                    idx = 2 * j + half
                    par = idx % 4
                    if idx >= 4:
                        t.wait_ge(a_v.h, marks[("v_evx", idx - 4)])
                    t.wait_ge(a_s.h, sA(j, 5))
                    for kk in range(8):
                        k = half * 8 + kk
                        ins = t.transpose(psA[par][:, kk * P:(kk + 1) * P],
                                          xnbt[bj][:, k * P:(k + 1) * P], idn_sb[:])
                    marks[("t_trx", idx)] = a_t.inc(ins)

            @blk.vector
            def _(v):
                for half in range(2):
                    idx = 2 * j + half
                    par = idx % 4
                    v.wait_ge(a_t.h, marks[("t_trx", idx)])
                    dst = xt_view[:, half * 8:(half + 1) * 8, j * P:(j + 1) * P]
                    srcv = psA[par][:].rearrange("p (k t) -> p k t", k=8)
                    marks[("v_evx", idx)] = a_v.inc(v.tensor_copy(dst, srcv))

            @blk.gpsimd
            def _(g):
                g.wait_ge(a_s.h, sA(j, 7))
                if j >= 3:
                    g.wait_ge(a_t.h, marks[("t_trq", 2 * (j - 3) + 1)])
                marks[("g_qb", j)] = a_g.inc(
                    g.dma_start(out=qbt[j % 3][:], in_=qi8[bj][:]), 16)

            @blk.tensor
            def _(t):
                for half in range(2):
                    idx = 2 * j + half
                    par = idx % 4
                    if idx >= 4:
                        if half == 0:
                            t.wait_ge(a_s.h, sEvq0(j - 2))
                        else:
                            t.wait_ge(a_v.h, marks[("v_eq1", j - 2)])
                    t.wait_ge(a_g.h, marks[("g_qb", j)])
                    for kk in range(8):
                        k = half * 8 + kk
                        ins = t.transpose(psQ[par][:, kk * P:(kk + 1) * P],
                                          qbt[j % 3][:, k * P:(k + 1) * P], idn_sb[:])
                    marks[("t_trq", idx)] = a_t.inc(ins)

            @blk.gpsimd
            def _(g):
                for half in range(2):
                    g.wait_ge(a_v.h, marks[("v_evx", 2 * j + half)])
                marks[("g_wx", j)] = a_wx[bj].inc(
                    g.dma_start(out=xt2_d[j], in_=xt_view[:, :, j * P:(j + 1) * P]), 16)

        def emit_evq(j):
            bj = j % 2

            @blk.vector
            def _(v):
                idx = 2 * j + 1
                v.wait_ge(a_t.h, marks[("t_trq", idx)])
                if j >= 2:
                    v.wait_ge(a_wq[bj].h, marks[("g_wq", j - 2)])
                marks[("v_eq1", j)] = a_v.inc(
                    v.tensor_copy(qT_st[bj][:, 1024:2048], psQ[idx % 4][:]))

            @blk.gpsimd
            def _(g):
                g.wait_ge(a_s.h, sEvq0(j))
                g.wait_ge(a_v.h, marks[("v_eq1", j)])
                marks[("g_wq", j)] = a_wq[bj].inc(
                    g.dma_start(out=qt2_d[j, :, :], in_=qT_st[bj][:]), 16)

        for i in range(NT):
            b = i % 2
            b3 = i % 3

            @blk.sync
            def _(sy):
                if i >= 3:
                    sy.wait_ge(a_s.h, sA(i - 3, 4))
                marks[("xld", i)] = a_xld[b3].inc(
                    sy.dma_start(out=xf[b3][:], in_=x_d[i * P:(i + 1) * P, :]), 16)

            @blk.vector
            def _(v):
                st = stt[b3]
                v.wait_ge(a_xld[b3].h, marks[("xld", i)])
                if i >= 3:
                    v.wait_ge(a_s.h, sA(i - 3, 7))
                for k in range(4):
                    ins = v.bn_stats(bn6[b3][:, k * 6:(k + 1) * 6],
                                     xf[b3][:, k * 512:(k + 1) * 512])
                n = a_v.inc(ins)
                v.wait_ge(a_v.h, n)
                n = a_v.inc(v.bn_aggr(agg[b3][:], bn6[b3][:]))
                v.wait_ge(a_v.h, n)
                n = a_v.inc(v.tensor_scalar_add(st[VARP][:], agg[b3][:, 1:2], 1e-5))
                marks[("v_stat", i)] = a_v.inc(
                    v.tensor_scalar_mul(st[NMRS][:], agg[b3][:, 0:1], -1.0))

            if i >= 1:
                emit_tail(i - 1)
            if i >= 2:
                emit_evq(i - 2)
        emit_tail(NT - 1)
        emit_evq(NT - 2)
        emit_evq(NT - 1)

        def s_evq(s, j):
            bj = j % 2
            if j >= 2:
                s.wait_ge(a_wq[bj].h, marks[("g_wq", j - 2)])
            idx = 2 * j
            s.wait_ge(a_t.h, marks[("t_trq", idx)])
            n = a_s.inc(s.copy(qT_st[bj][:, 0:1024], psQ[idx % 4][:]))
            assert n == sEvq0(j)

        @blk.scalar
        def _(s):
            for i in range(NT):
                b = i % 2
                st = stt[i % 3]
                s.wait_ge(a_v.h, marks[("v_stat", i)])
                n = a_s.inc(s.activation(st[RSTD][:], st[VARP][:], AF.Ln))
                s.wait_ge(a_s.h, n)
                n = a_s.inc(s.activation(st[RSTD][:], st[RSTD][:], AF.Exp, scale=-0.5))
                s.wait_ge(a_s.h, n)
                n = a_s.inc(s.activation(st[NMRS][:], st[NMRS][:], AF.Identity,
                                         scale=st[RSTD][:]))
                s.wait_ge(a_s.h, n)
                if i >= 2:
                    s.wait_ge(a_v.h, marks[("v_rsc", i - 2)])
                n = a_s.inc(s.activation(xnf[b][:], xf[i % 3][:], AF.Identity,
                                         bias=st[NMRS][:], scale=st[RSTD][:]))
                assert n == sA(i, 4)
                if i >= 2:
                    s.wait_ge(a_t.h, marks[("t_trx", 2 * (i - 2) + 1)])
                a_s.inc(s.copy(xnbt[b][:], xnf[b][:]))
                if i >= 1:
                    s_evq(s, i - 1)
                s.wait_ge(a_v.h, marks[("v_rsc", i)])
                if i >= 2:
                    s.wait_ge(a_g.h, marks[("g_qb", i - 2)])
                n = a_s.inc(s.activation(qi8[b][:], xnf[b][:], AF.Identity,
                                         scale=st[AM][:]))
                assert n == sA(i, 7)
            s_evq(s, NT - 1)

        if debug:
            @blk.sync
            def _(sy):
                sy.wait_ge(a_v.h, marks[("v_rsc", NT - 1)])
                cst.inc(sy.dma_start(out=is_d[:, :], in_=is_sb[:]), 16)
                sy.wait_ge(cst.h, cst.n)

        barA_n = barrier(barA, ("vector", "scalar", "tensor", "gpsimd"))
        esA2.close()

    barB = Sem(nc, es, "barB")
    barB_n = 0
    if "B" in phases:
        esB = ExitStack()
        wqb = [esB.enter_context(nc.sbuf_tensor(f"wqb{j}", [P, D], BF16)) for j in range(2)]
        stg = [esB.enter_context(nc.sbuf_tensor(f"stg{j}", [P, D], BF16)) for j in range(2)]
        psB = [esB.enter_context(nc.psum_tensor(f"psB{j}", [P, D], F32)) for j in range(2)]
        bqkc_sb = esB.enter_context(nc.sbuf_tensor("bqkc_sb", [P, 32], F32))
        bqkc2_sb = esB.enter_context(nc.sbuf_tensor("bqkc2_sb", [P, 32], F32))

        @blk.sync
        def _(sy):
            sy.wait_ge(barA.h, barA_n)
            cst.inc(sy.dma_start(out=bqkc_sb[:], in_=bqkc_d[:, :]), 16)
        cstB = cst.n
        b_prep = Sem(nc, es, "b_prep")

        @blk.vector
        def _(v):
            v.wait_ge(barA.h, barA_n)
            v.wait_ge(cst.h, cstB)
            n = b_prep.inc(v.tensor_scalar_mul(bqkc2_sb[:, 0:NH], bqkc_sb[:, 0:NH],
                                               1.0 / 128.0))
            v.wait_ge(b_prep.h, n)
            b_prep.inc(v.tensor_copy(bqkc2_sb[:, NH:32], bqkc_sb[:, NH:32]))
        b_prep_n = b_prep.n

        b_wld = [Sem(nc, es, f"b_wld{j}") for j in range(2)]
        b_mm = Sem(nc, es, "b_mm")
        b_ev = Sem(nc, es, "b_ev")
        b_scrw = [Sem(nc, es, f"b_scrw{j}") for j in range(2)]

        for c in range(32):
            @blk.gpsimd
            def _(g):
                if c == 0:
                    g.wait_ge(barA.h, barA_n)
                    for sp in a_wx + a_wq:
                        g.wait_ge(sp.h, sp.n)
                if c >= 2:
                    g.wait_ge(b_mm.h, marks[("t_b", c - 2, 1)])
                marks[("b_wld", c)] = b_wld[c % 2].inc(
                    g.dma_start(out=wqb[c % 2][:], in_=wq_d[c, :, :]), 16)

            for th in range(2):
                par = (2 * c + th) % 2
                idx = 2 * c + th

                @blk.tensor
                def _(t):
                    if idx == 0:
                        t.wait_ge(barA.h, barA_n)
                    t.wait_ge(b_wld[c % 2].h, marks[("b_wld", c)])
                    if idx >= 2:
                        t.wait_ge(b_ev.h, idx - 1)
                    ins = None
                    for k in range(NK):
                        for nb in range(4):
                            ins = t.matmul(
                                psB[par][:, nb * 512:(nb + 1) * 512],
                                wqb[c % 2][:, k * P:(k + 1) * P],
                                xt_sb[:, k * T + th * D + nb * 512:
                                      k * T + th * D + (nb + 1) * 512],
                                start=(k == 0), stop=(k == NK - 1))
                    marks[("t_b", c, th)] = b_mm.inc(ins)

                @blk.sync
                def _(sy):
                    sy.wait_ge(b_ev.h, idx + 1)
                    dst = (qth_d if c < NH else kth_d)[c % NH, :, th * D:(th + 1) * D]
                    marks[("b_w", idx)] = b_scrw[par].inc(
                        sy.dma_start(out=dst, in_=stg[par][:]), 16)

        @blk.scalar
        def _(s):
            s.wait_ge(b_prep.h, b_prep_n)
            for c in range(32):
                for th in range(2):
                    par = (2 * c + th) % 2
                    idx = 2 * c + th
                    s.wait_ge(b_mm.h, marks[("t_b", c, th)])
                    if idx >= 2:
                        s.wait_ge(b_scrw[par].h, marks[("b_w", idx - 2)])
                    n = b_ev.inc(
                        s.activation(stg[par][:], psB[par][:], AF.Identity,
                                     scale=(1.0 / 128.0 if c < NH else 1.0),
                                     bias=bqkc2_sb[:, c:c + 1]))
                    assert n == idx + 1

        @blk.sync
        def _(sy):
            for sp in b_scrw:
                sy.wait_ge(sp.h, sp.n)
        barB_n = barrier(barB, ("tensor", "scalar", "sync", "gpsimd"))
        esB.close()
    esA.close()

    barV = Sem(nc, es, "barV")
    barV_n = 0
    if "V" in phases:
        esV = ExitStack()
        wft_sb = esV.enter_context(nc.sbuf_tensor("wft_sb", [P, NK * D], BF16))
        wvr_sb = esV.enter_context(nc.sbuf_tensor("wvr_sb", [P, NK * D], BF16))
        ws_bc = esV.enter_context(nc.sbuf_tensor("ws_bc", [P, D], F32))
        fb_bc = esV.enter_context(nc.sbuf_tensor("fb_bc", [P, D], F32))
        xts = [esV.enter_context(nc.sbuf_tensor(f"xts{j}", [P, D], BF16)) for j in range(2)]
        qts = [esV.enter_context(nc.sbuf_tensor(f"qts{j}", [P, D], BF16)) for j in range(2)]
        vst = [esV.enter_context(nc.sbuf_tensor(f"vst{j}", [P, D], BF16)) for j in range(2)]
        ff1 = esV.enter_context(nc.sbuf_tensor("ff1", [P, D], F32))
        gg16 = [esV.enter_context(nc.sbuf_tensor(f"gg16_{j}", [P, D], F16)) for j in range(2)]
        psF = esV.enter_context(nc.psum_tensor("psF", [P, D], F32))
        psV = esV.enter_context(nc.psum_tensor("psV", [P, D], F32))
        vec1_sb = esV.enter_context(nc.sbuf_tensor("vec1_sb", [1, D], F32))

        v_wld = Sem(nc, es, "v_wld")
        v_lx = [Sem(nc, es, f"v_lx{j}") for j in range(2)]
        v_lq = [Sem(nc, es, f"v_lq{j}") for j in range(2)]
        v_mm = Sem(nc, es, "v_mm")
        v_evv = Sem(nc, es, "v_evv")
        v_evs = Sem(nc, es, "v_evs")
        g_gw = [Sem(nc, es, f"g_gw{j}") for j in range(2)]
        g_vw = [Sem(nc, es, f"g_vw{j}") for j in range(2)]

        @blk.sync
        def _(sy):
            sy.wait_ge(barB.h, barB_n)
            cst.inc(sy.dma_start(out=vec1_sb[:], in_=wsc_d[:, :]), 16)
        cstV = cst.n

        @blk.gpsimd
        def _(g):
            g.wait_ge(barB.h, barB_n)
            for k in range(NK):
                v_wld.inc(g.dma_start(out=wft_sb[:, k * D:(k + 1) * D],
                                      in_=wf_d[k * P:(k + 1) * P, :]), 16)
            for k in range(NK):
                v_wld.inc(g.dma_start(out=wvr_sb[:, k * D:(k + 1) * D],
                                      in_=wv_d[k, :, :]), 16)
        v_wld_done = v_wld.n

        @blk.tensor
        def _(t):
            t.wait_ge(barB.h, barB_n)
            t.wait_ge(cst.h, cstV)
            for nb in range(4):
                ins = t.matmul(psF[:, nb * 512:(nb + 1) * 512],
                               onr_sb[0:1, 0:P], vec1_sb[0:1, nb * 512:(nb + 1) * 512],
                               start=True, stop=True)
            marks[("v_bc1",)] = v_mm.inc(ins)

        @blk.vector
        def _(v):
            v.wait_ge(v_mm.h, marks[("v_bc1",)])
            marks[("v_bc1c",)] = v_evv.inc(v.tensor_copy(ws_bc[:], psF[:]))

        @blk.sync
        def _(sy):
            sy.wait_ge(v_mm.h, marks[("v_bc1",)])
            cst.inc(sy.dma_start(out=vec1_sb[:], in_=fbi_d[:, :]), 16)
        cstV2 = cst.n

        @blk.tensor
        def _(t):
            t.wait_ge(v_evv.h, marks[("v_bc1c",)])
            t.wait_ge(cst.h, cstV2)
            for nb in range(4):
                ins = t.matmul(psF[:, nb * 512:(nb + 1) * 512],
                               onr_sb[0:1, 0:P], vec1_sb[0:1, nb * 512:(nb + 1) * 512],
                               start=True, stop=True)
            marks[("v_bc2",)] = v_mm.inc(ins)

        @blk.vector
        def _(v):
            v.wait_ge(v_mm.h, marks[("v_bc2",)])
            marks[("v_bc2c",)] = v_evv.inc(v.tensor_copy(fb_bc[:], psF[:]))
        ffn_psF_ready = marks[("v_bc2c",)]

        for i in range(NT):
            b = i % 2

            @blk.sync
            def _(sy):
                if i == 0:
                    sy.wait_ge(barB.h, barB_n)
                    for sp in a_wx + a_wq:
                        sy.wait_ge(sp.h, sp.n)
                if i >= 2:
                    sy.wait_ge(v_mm.h, marks[("t_v", i - 2)])
                marks[("v_lx", i)] = v_lx[b].inc(
                    sy.dma_start(out=xts[b][:].rearrange("p (k t) -> p k t", k=NK),
                                 in_=xt2_d[i]), 16)
                if i >= 2:
                    sy.wait_ge(v_mm.h, marks[("t_f", i - 2)])
                marks[("v_lq", i)] = v_lq[b].inc(
                    sy.dma_start(out=qts[b][:], in_=qt2_d[i, :, :]), 16)

            @blk.tensor
            def _(t):
                t.wait_ge(v_lq[b].h, marks[("v_lq", i)])
                if i == 0:
                    t.wait_ge(v_evv.h, ffn_psF_ready)
                else:
                    t.wait_ge(v_evv.h, marks[("v_f1", i - 1)])
                ins = None
                for k in range(NK):
                    if i == 0:
                        t.wait_ge(v_wld.h, 16 * (k + 1))
                    for nb in range(4):
                        ins = t.matmul(psF[:, nb * 512:(nb + 1) * 512],
                                       qts[b][:, k * P:(k + 1) * P],
                                       wft_sb[:, k * D + nb * 512:k * D + (nb + 1) * 512],
                                       start=(k == 0), stop=(k == NK - 1))
                marks[("t_f", i)] = v_mm.inc(ins)
                t.wait_ge(v_lx[b].h, marks[("v_lx", i)])
                if i >= 1:
                    t.wait_ge(v_evs.h, 3 * i)
                for k in range(NK):
                    if i == 0:
                        t.wait_ge(v_wld.h, 16 * (NK + k + 1))
                    for nb in range(4):
                        ins = t.matmul(psV[:, nb * 512:(nb + 1) * 512],
                                       xts[b][:, k * P:(k + 1) * P],
                                       wvr_sb[:, k * D + nb * 512:k * D + (nb + 1) * 512],
                                       start=(k == 0), stop=(k == NK - 1))
                marks[("t_v", i)] = v_mm.inc(ins)

            @blk.vector
            def _(v):
                v.wait_ge(v_mm.h, marks[("t_f", i)])
                if i >= 1:
                    v.wait_ge(v_evs.h, 3 * i - 2)
                marks[("v_f1", i)] = v_evv.inc(
                    v.tensor_scalar(ff1[:], in0=psF[:], scalar1=is_sb[:, i:i + 1],
                                    scalar2=None, op0=ALU.mult))
                v.wait_ge(v_evv.h, marks[("v_f1", i)])
                n = v_evv.inc(v.tensor_tensor(ff1[:], ff1[:], ws_bc[:], op=ALU.mult))
                v.wait_ge(v_evv.h, n)
                marks[("v_f3", i)] = v_evv.inc(
                    v.tensor_tensor(ff1[:], ff1[:], fb_bc[:], op=ALU.add))

            @blk.gpsimd
            def _(g):
                g.wait_ge(v_evs.h, 3 * i + 2)
                marks[("g_gw", i)] = g_gw[b].inc(
                    g.dma_start(out=g2_d[i, :, :], in_=gg16[b][:]), 16)
                g.wait_ge(v_evs.h, 3 * i + 3)
                marks[("g_vw", i)] = g_vw[b].inc(
                    g.dma_start(out=vn2_d[i, :, :], in_=vst[b][:]), 16)

        @blk.scalar
        def _(s):
            for i in range(NT):
                b = i % 2
                s.wait_ge(v_evv.h, marks[("v_f3", i)])
                if i >= 2:
                    s.wait_ge(g_gw[b].h, marks[("g_gw", i - 2)])
                n = v_evs.inc(s.copy(gg16[b][:], ff1[:]))
                s.wait_ge(v_evs.h, n)
                n = v_evs.inc(s.activation(gg16[b][:], gg16[b][:], AF.Silu))
                assert n == 3 * i + 2
                s.wait_ge(v_mm.h, marks[("t_v", i)])
                if i >= 2:
                    s.wait_ge(g_vw[b].h, marks[("g_vw", i - 2)])
                n = v_evs.inc(s.copy(vst[b][:], psV[:]))
                assert n == 3 * i + 3

        @blk.gpsimd
        def _(g):
            for sp in g_gw + g_vw:
                g.wait_ge(sp.h, sp.n)
        barV_n = barrier(barV, ("tensor", "scalar", "vector", "gpsimd", "sync"))
        esV.close()

    barC = Sem(nc, es, "barC")
    barC_n = 0
    c_yw = [Sem(nc, es, f"c_yw{j}") for j in range(2)]
    wo_lo = es.enter_context(nc.sbuf_tensor("wo_lo", [P, 6 * D], BF16))
    f_wo = Sem(nc, es, "f_wo")
    f_wo2 = Sem(nc, es, "f_wo2")
    if "C" in phases:
        esC = ExitStack()
        kt_sb = [esC.enter_context(nc.sbuf_tensor(f"kt{j}", [P, T], BF16)) for j in range(3)]
        vt_sb = [esC.enter_context(nc.sbuf_tensor(f"vt{j}", [P, T], BF16)) for j in range(3)]
        qg_sb = [esC.enter_context(nc.sbuf_tensor(f"qg{j}", [P, NH * 512], BF16)) for j in range(2)]
        ytg_sb = [esC.enter_context(nc.sbuf_tensor(f"ytg{j}", [P, NH * 512], BF16)) for j in range(2)]
        pt_sb = [esC.enter_context(nc.sbuf_tensor(f"pt{j}", [P, NT * 512], BF16)) for j in range(2)]
        rl_sb = esC.enter_context(nc.sbuf_tensor("rl_sb", [1, 512], F32))
        bc_sb = esC.enter_context(nc.sbuf_tensor("bc_sb", [P, 512], F32))
        psS = [esC.enter_context(nc.psum_tensor(f"psS{j}", [P, 1024], F32)) for j in range(2)]
        psY = [esC.enter_context(nc.psum_tensor(f"psY{j}", [P, 512], F32)) for j in range(2)]
        psDB = [esC.enter_context(nc.psum_tensor(f"psDB{j}", [P, 512], F32)) for j in range(2)]

        c_kld = [Sem(nc, es, f"c_kld{j}") for j in range(3)]
        c_vld = [Sem(nc, es, f"c_vld{j}") for j in range(3)]
        c_qld = [Sem(nc, es, f"c_qld{j}") for j in range(2)]
        c_mm = Sem(nc, es, "c_mm")
        c_ex = Sem(nc, es, "c_ex")
        c_ve = Sem(nc, es, "c_ve")

        steps = [(g8, h) for g8 in range(NG) for h in range(NH)]
        NP = len(steps)
        gbase = {}
        gctr = [0]

        def jmax_of(pos):
            return GQ * steps[pos][0] + GQ

        def ntrip_of(pos):
            return (jmax_of(pos) + 1) // 2

        def emit_loads(pos):
            g8, h = steps[pos]
            jmax = jmax_of(pos)
            b = pos % 3

            @blk.sync
            def _(sy):
                if pos == 0:
                    sy.wait_ge(barV.h, barV_n)
                if pos >= 3:
                    sy.wait_ge(c_mm.h, max(marks[("c_dn", pos - 3)],
                                           marks[("c_pv", pos - 3)]))
                marks[("c_k", pos)] = c_kld[b].inc(
                    sy.dma_start(out=kt_sb[b][:, 0:jmax * P],
                                 in_=kth_d[h, :, 0:jmax * P]), 16)
                qload = [0] if pos == 0 else (
                    [g8 + 1] if (h == 2 and g8 + 1 < NG) else [])
                for gq in qload:
                    gb2 = gq % 2
                    if gq >= 2:
                        sy.wait_ge(c_mm.h, max(marks[("c_dn", (gq - 1) * NH - 1)],
                                               marks[("c_pv", (gq - 1) * NH - 1)]))
                    srcq = qth_d[:, :, gq * 512:(gq + 1) * 512].rearrange("h p t -> p h t")
                    dstq = qg_sb[gb2][:].rearrange("p (h t) -> p h t", h=NH)
                    marks[("c_q", gq)] = c_qld[gb2].inc(
                        sy.dma_start(out=dstq, in_=srcq), 16)

            @blk.gpsimd
            def _(g):
                if pos == 0:
                    g.wait_ge(barV.h, barV_n)
                if pos >= 3:
                    g.wait_ge(c_mm.h, max(marks[("c_dn", pos - 3)],
                                          marks[("c_pv", pos - 3)]))
                srcv = vn2_d[0:jmax, :, h * P:(h + 1) * P].rearrange("j p d -> p j d")
                dstv = vt_sb[b][:, 0:jmax * P].rearrange("p (j d) -> p j d", j=jmax)
                marks[("c_v", pos)] = c_vld[b].inc(g.dma_start(out=dstv, in_=srcv), 16)
                if 16 <= pos < 22:
                    k = pos - 16
                    f_wo.inc(g.dma_start(out=wo_lo[:, k * D:(k + 1) * D],
                                         in_=wo_d[k * P:(k + 1) * P, :]), 16)

        def emit_S_triple(t, pos, tr):
            g8, h = steps[pos]
            jmax = jmax_of(pos)
            b = pos % 3
            gb = g8 % 2
            g = gbase[pos] + tr
            if g >= 2:
                t.wait_ge(c_ex.h, g - 1)
            ins = None
            for jj in range(2):
                j = tr * 2 + jj
                if j >= jmax:
                    break
                ins = t.matmul(psS[g % 2][:, jj * 512:(jj + 1) * 512],
                               kt_sb[b][:, j * P:(j + 1) * P],
                               qg_sb[gb][:, h * 512:(h + 1) * 512],
                               start=True, stop=True)
            marks[("c_st", pos, tr)] = c_mm.inc(ins)

        def emit_jrange(t, pos, jlo, jhi, jmaxp, which):
            g8, h = steps[pos]
            b = pos % 3
            diag = GQ * g8
            t.wait_ge(c_ex.h, gbase[pos] + (jhi - 1) // 2 + 1)
            if jlo == 0:
                t.wait_ge(c_vld[pos % 3].h, marks[("c_v", pos)])
                if pos >= 2:
                    t.wait_ge(c_ve.h, marks[("c_ye", pos - 2)])
            if jlo <= diag < jhi:
                t.wait_ge(c_ve.h, marks[("c_msk", pos)])
            ins = None
            for j in range(jlo, jhi):
                s0 = max(0, j - diag) * P
                if which == "pv":
                    ins = t.matmul(psY[pos % 2][:, s0:512],
                                   vt_sb[b][:, j * P:(j + 1) * P],
                                   pt_sb[pos % 2][:, j * 512 + s0:(j + 1) * 512],
                                   start=(j == 0), stop=(j == jmaxp - 1))
                else:
                    ins = t.matmul(psDB[pos % 2][:, s0:512], onm_sb[:],
                                   pt_sb[pos % 2][:, j * 512 + s0:(j + 1) * 512],
                                   start=(j == 0), stop=(j == jmaxp - 1))
            if jhi == jmaxp:
                marks[("c_pv" if which == "pv" else "c_dn", pos)] = c_mm.inc(ins)

        def emit_step(pos):
            g8, h = steps[pos]
            b = pos % 3
            ntrip = ntrip_of(pos)
            gbase[pos] = gctr[0]
            gctr[0] += ntrip
            jmax1 = jmax_of(pos - 1) if pos >= 1 else 0

            @blk.tensor
            def _(t):
                if pos == 0:
                    t.wait_ge(barV.h, barV_n)
                t.wait_ge(c_kld[b].h, marks[("c_k", pos)])
                t.wait_ge(c_qld[g8 % 2].h, marks[("c_q", g8)])
                H = max(1, (ntrip + 1) // 2)
                for tr in range(ntrip):
                    if pos >= 1:
                        if tr < H:
                            dlo = tr * jmax1 // H
                            dhi = (tr + 1) * jmax1 // H
                            if dlo < dhi:
                                emit_jrange(t, pos - 1, dlo, dhi, jmax1, "dn")
                        jlo = tr * jmax1 // ntrip
                        jhi = (tr + 1) * jmax1 // ntrip
                        if jlo < jhi:
                            emit_jrange(t, pos - 1, jlo, jhi, jmax1, "pv")
                    emit_S_triple(t, pos, tr)

            if pos >= 1:
                @blk.vector
                def _(v):
                    v.wait_ge(c_mm.h, marks[("c_dn", pos - 1)])
                    marks[("c_rc", pos - 1)] = c_ve.inc(
                        v.reciprocal(rl_sb[:], psDB[(pos - 1) % 2][0:1, :]))

        def emit_final_pvdn(pos):
            jmax1 = jmax_of(pos)

            @blk.tensor
            def _(t):
                emit_jrange(t, pos, 0, jmax1, jmax1, "dn")
                emit_jrange(t, pos, 0, jmax1, jmax1, "pv")

            @blk.vector
            def _(v):
                v.wait_ge(c_mm.h, marks[("c_dn", pos)])
                marks[("c_rc", pos)] = c_ve.inc(
                    v.reciprocal(rl_sb[:], psDB[pos % 2][0:1, :]))

        def emit_mask(pos):
            g8, h = steps[pos]

            @blk.vector
            def _(v):
                ins = None
                for sub in range(GQ):
                    j = GQ * g8 + sub
                    v.wait_ge(c_ex.h, gbase[pos] + j // 2 + 1)
                    sl = pt_sb[pos % 2][:, j * 512 + sub * P: j * 512 + (sub + 1) * P]
                    ins = v.tensor_tensor(sl, sl, tri_sb[:], op=ALU.mult)
                marks[("c_msk", pos)] = c_ve.inc(ins)

        def emit_BC(pos):
            @blk.tensor
            def _(t):
                t.wait_ge(c_ve.h, marks[("c_rc", pos)])
                marks[("c_bc", pos)] = c_mm.inc(
                    t.matmul(psDB[pos % 2][:, 0:512], onr_sb[0:1, 0:P], rl_sb[0:1, :],
                             start=True, stop=True))

        def emit_evac(pos):
            g8, h = steps[pos]
            gb = g8 % 2

            @blk.vector
            def _(v):
                v.wait_ge(c_mm.h, marks[("c_bc", pos)])
                n = c_ve.inc(v.tensor_copy(bc_sb[:], psDB[pos % 2][:]))
                v.wait_ge(c_mm.h, marks[("c_pv", pos)])
                if h == 0 and g8 >= 2:
                    v.wait_ge(c_yw[g8 % 2].h, marks[("c_ytw", g8 - 2)])
                v.wait_ge(c_ve.h, n)
                n = c_ve.inc(
                    v.tensor_tensor(ytg_sb[gb][:, h * 512:(h + 1) * 512],
                                    psY[pos % 2][:], bc_sb[:], op=ALU.mult))
                v.wait_ge(c_ve.h, n)
                marks[("c_ye", pos)] = c_ve.inc(
                    v.tensor_scalar_add(ytg_sb[gb][:, h * 512:(h + 1) * 512],
                                        ytg_sb[gb][:, h * 512:(h + 1) * 512],
                                        bqvc_sb[:, h:h + 1]))

            if h == NH - 1:
                @blk.sync
                def _(sy):
                    sy.wait_ge(c_ve.h, marks[("c_ye", pos)])
                    dstv = yt_d[:, :, g8 * 512:(g8 + 1) * 512].rearrange("h p t -> p h t")
                    srcv = ytg_sb[gb][:].rearrange("p (h t) -> p h t", h=NH)
                    marks[("c_ytw", g8)] = c_yw[gb].inc(sy.dma_start(out=dstv, in_=srcv), 16)

        def emit_exp_all():
            @blk.scalar
            def _(s):
                for pos in range(NP):
                    jmax = jmax_of(pos)
                    ntrip = ntrip_of(pos)
                    for tr in range(ntrip):
                        g = gbase[pos] + tr
                        nj = min(2, jmax - tr * 2)
                        s.wait_ge(c_mm.h, marks[("c_st", pos, tr)])
                        assert c_ex.n == g, "exp order mismatch"
                        c_ex.inc(s.activation(
                            pt_sb[pos % 2][:, tr * 1024:tr * 1024 + nj * 512],
                            psS[g % 2][:, 0:nj * 512], AF.Exp))

        emit_loads(0)
        for pos in range(NP):
            if pos >= 2:
                emit_BC(pos - 2)
                emit_evac(pos - 2)
            emit_step(pos)
            emit_mask(pos)
            if pos + 1 < NP:
                emit_loads(pos + 1)
        emit_BC(NP - 2)
        emit_evac(NP - 2)
        emit_final_pvdn(NP - 1)
        emit_BC(NP - 1)
        emit_evac(NP - 1)
        emit_exp_all()

        @blk.sync
        def _(sy):
            for sp in c_yw:
                sy.wait_ge(sp.h, sp.n)
        barC_n = barrier(barC, ("tensor", "scalar", "vector", "gpsimd", "sync"))
        esC.close()

    if "F" in phases:
        esF = ExitStack()
        wo_hi = esF.enter_context(nc.sbuf_tensor("wo_hi", [P, 10 * D], BF16))
        onmf_sb = esF.enter_context(nc.sbuf_tensor("onmf_sb", [P, P], F32))
        ytb = [esF.enter_context(nc.sbuf_tensor(f"ytb{j}", [P, D], BF16)) for j in range(2)]
        gg2 = [esF.enter_context(nc.sbuf_tensor(f"gg2_{j}", [P, D], F16)) for j in range(2)]
        gfb = esF.enter_context(nc.sbuf_tensor("gfb", [P, D], F32))
        xf2 = [esF.enter_context(nc.sbuf_tensor(f"xf2_{j}", [P, D], F32)) for j in range(2)]
        outb = [esF.enter_context(nc.sbuf_tensor(f"outb{j}", [P, D], F32)) for j in range(2)]
        psO = [esF.enter_context(nc.psum_tensor(f"psO{j}", [P, D], F32)) for j in range(2)]
        bout_sb = esF.enter_context(nc.sbuf_tensor("bout_sb", [1, D], F32))
        bob_bc = esF.enter_context(nc.sbuf_tensor("bob_bc", [P, D], F32))

        @blk.sync
        def _(sy):
            sy.wait_ge(barC.h, barC_n)
            cst.inc(sy.dma_start(out=bout_sb[:], in_=bout_d[:, :]), 16)
            cst.inc(sy.dma_start(out=onmf_sb[:], in_=onmf_d[:, :]), 16)
        cstF = cst.n
        f_prep = Sem(nc, es, "f_prep")
        f_prepv = Sem(nc, es, "f_prepv")

        @blk.tensor
        def _(t):
            t.wait_ge(barC.h, barC_n)
            t.wait_ge(cst.h, cstF)
            for nb in range(4):
                ins = t.matmul(psO[0][:, nb * 512:(nb + 1) * 512], oni_sb[:],
                               bout_sb[0:1, nb * 512:(nb + 1) * 512],
                               start=True, stop=True)
            f_prep.inc(ins)
        f_prep_n = f_prep.n

        @blk.vector
        def _(v):
            v.wait_ge(f_prep.h, f_prep_n)
            f_prepv.inc(v.tensor_copy(bob_bc[:], psO[0][:]))
        f_prepv_n = f_prepv.n

        f_x = [Sem(nc, es, f"f_x{j}") for j in range(2)]
        f_g = [Sem(nc, es, f"f_g{j}") for j in range(2)]
        f_y = [Sem(nc, es, f"f_y{j}") for j in range(2)]
        f_mm = Sem(nc, es, "f_mm")
        f_cb = Sem(nc, es, "f_cb")
        f_out = [Sem(nc, es, f"f_out{j}") for j in range(2)]

        @blk.gpsimd
        def _(g):
            g.wait_ge(barC.h, barC_n)
            for sp in c_yw:
                g.wait_ge(sp.h, sp.n)
            if "C" not in phases:
                for k in range(6):
                    f_wo.inc(g.dma_start(out=wo_lo[:, k * D:(k + 1) * D],
                                         in_=wo_d[k * P:(k + 1) * P, :]), 16)
        f_wo_done = 16 * 11
        f_wo2_done = 16 * 5

        for i in range(NT):
            b = i % 2

            @blk.sync
            def _(sy):
                if i == 0:
                    sy.wait_ge(barC.h, barC_n)
                if i >= 2:
                    sy.wait_ge(f_cb.h, marks[("f_cb", i - 2)])
                marks[("f_x", i)] = f_x[b].inc(
                    sy.dma_start(out=xf2[b][:], in_=x_d[i * P:(i + 1) * P, :]), 16)
                marks[("f_g", i)] = f_g[b].inc(
                    sy.dma_start(out=gg2[b][:], in_=g2_d[i, :, :]), 16)

            @blk.gpsimd
            def _(g):
                if i >= 2:
                    g.wait_ge(f_mm.h, marks[("f_mm", i - 2)])
                src = yt_d[:, :, i * P:(i + 1) * P].rearrange("h p t -> p h t")
                dstv = ytb[b][:].rearrange("p (h t) -> p h t", h=NH)
                marks[("f_y", i)] = f_y[b].inc(g.dma_start(out=dstv, in_=src), 16)
                if i == 0:
                    for k in range(10):
                        sem = f_wo if k < 5 else f_wo2
                        sem.inc(g.dma_start(out=wo_hi[:, k * D:(k + 1) * D],
                                            in_=wo_d[(k + 6) * P:(k + 7) * P, :]), 16)

            @blk.tensor
            def _(t):
                if i == 0:
                    t.wait_ge(f_prepv.h, f_prepv_n)
                else:
                    t.wait_ge(f_wo.h, f_wo_done)
                    t.wait_ge(f_wo2.h, f_wo2_done)
                t.wait_ge(f_y[b].h, marks[("f_y", i)])
                if i >= 2:
                    t.wait_ge(f_cb.h, marks[("f_cb", i - 2)])
                for nb in range(4):
                    t.matmul(psO[b][:, nb * 512:(nb + 1) * 512], onmf_sb[:],
                             bob_bc[:, nb * 512:(nb + 1) * 512],
                             start=True, stop=False)
                ins = None
                for hh in range(NH):
                    if i == 0 and hh == 0:
                        t.wait_ge(f_wo.h, 16 * 6)
                    if i == 0 and hh == 6:
                        t.wait_ge(f_wo.h, f_wo_done)
                    if i == 0 and hh == 11:
                        t.wait_ge(f_wo2.h, f_wo2_done)
                    wsel, hsel = (wo_lo, hh) if hh < 6 else (wo_hi, hh - 6)
                    for nb in range(4):
                        ins = t.matmul(psO[b][:, nb * 512:(nb + 1) * 512],
                                       ytb[b][:, hh * P:(hh + 1) * P],
                                       wsel[:, hsel * D + nb * 512:hsel * D + (nb + 1) * 512],
                                       start=False, stop=(hh == NH - 1))
                marks[("f_mm", i)] = f_mm.inc(ins)

            @blk.vector
            def _(v):
                v.wait_ge(f_g[b].h, marks[("f_g", i)])
                n = f_cb.inc(v.tensor_copy(gfb[:], gg2[b][:]))
                v.wait_ge(f_mm.h, marks[("f_mm", i)])
                if i >= 2:
                    v.wait_ge(f_out[b].h, marks[("f_o", i - 2)])
                v.wait_ge(f_cb.h, n)
                n = f_cb.inc(v.tensor_tensor(outb[b][:], psO[b][:], gfb[:], op=ALU.mult))
                v.wait_ge(f_x[b].h, marks[("f_x", i)])
                v.wait_ge(f_cb.h, n)
                marks[("f_cb", i)] = f_cb.inc(
                    v.tensor_tensor(outb[b][:], outb[b][:], xf2[b][:], op=ALU.add))

            @blk.sync
            def _(sy):
                sy.wait_ge(f_cb.h, marks[("f_cb", i)])
                marks[("f_o", i)] = f_out[b].inc(
                    sy.dma_start(out=out_d[i * P:(i + 1) * P, :], in_=outb[b][:]), 16)

        @blk.sync
        def _(sy):
            for sp in f_out:
                sy.wait_ge(sp.h, sp.n)
        esF.close()

    es.close()
    return nc


def _host_inputs(L0_attn_x, L0_attn_W_qkv, L0_attn_b_qkv, L0_attn_W_out,
                 L0_attn_b_out, L0_ffn_weight, L0_ffn_w_scale, L0_ffn_bias):
    bf16 = ml_dtypes.bfloat16
    Wq = np.asarray(L0_attn_W_qkv, dtype=np.float32)
    wq = np.ascontiguousarray(
        Wq[:, :2 * D].reshape(NK, P, 32, P).transpose(2, 1, 0, 3))
    wv = np.ascontiguousarray(Wq[:, 2 * D:].reshape(NK, P, D))
    wo = np.ascontiguousarray(np.asarray(L0_attn_W_out, dtype=np.float32))
    wf = np.ascontiguousarray(np.asarray(L0_ffn_weight).T)
    bq3 = np.asarray(L0_attn_b_qkv, np.float32)
    bqkc = np.ascontiguousarray(bq3[:2 * D].reshape(32, P).T)
    bqvc = np.ascontiguousarray(bq3[2 * D:].reshape(NH, P).T)
    bout = np.ascontiguousarray(np.asarray(L0_attn_b_out, np.float32).reshape(1, D))
    wsc = np.ascontiguousarray(np.asarray(L0_ffn_w_scale, np.float32).reshape(-1)[None, :])
    fbi = np.ascontiguousarray(np.asarray(L0_ffn_bias, np.float32).reshape(1, D))
    idn = np.eye(P, dtype=bf16)
    tri = np.triu(np.ones((P, P), np.float32)).astype(bf16)
    onc = np.ones((P, 1), bf16)
    onm = np.ones((P, P), bf16)
    oni = np.full((1, P), 1.0 / 128.0, np.float32)
    onmf = np.ones((P, P), np.float32)
    onr = np.ones((1, 512), np.float32)
    shared = dict(wq=wq, wv=wv, wo=wo, wf=wf, bqkc=bqkc, bqvc=bqvc, bout=bout,
                  wsc=wsc, fbi=fbi, idn=idn, tri=tri, onc=onc, onm=onm, onr=onr,
                  oni=oni, onmf=onmf)
    xs = np.asarray(L0_attn_x, dtype=np.float32)
    maps = []
    for b in range(8):
        m = dict(shared)
        m["x"] = np.ascontiguousarray(xs[b])
        maps.append(m)
    return maps


def kernel(L0_attn_x, L0_attn_W_qkv, L0_attn_b_qkv, L0_attn_W_out,
           L0_attn_b_out, L0_ffn_weight, L0_ffn_w_scale, L0_ffn_bias):
    maps = _host_inputs(L0_attn_x, L0_attn_W_qkv, L0_attn_b_qkv, L0_attn_W_out,
                        L0_attn_b_out, L0_ffn_weight, L0_ffn_w_scale, L0_ffn_bias)
    nc = build_program()
    res = run_bass_kernel_spmd(nc, maps, core_ids=list(range(8)))
    return np.stack([np.asarray(r["out"]) for r in res.results], axis=0)
